# version 03
# baseline (speedup 1.0000x reference)
import sys
sys.path.insert(0, "/opt/trn_rl_repo")
from contextlib import ExitStack
from functools import partial

import numpy as np
import ml_dtypes

import concourse.bass as bass
import concourse.mybir as mybir
from concourse.bass_utils import run_bass_kernel_spmd

F32 = mybir.dt.float32
BF16 = mybir.dt.bfloat16
U8 = mybir.dt.uint8
AF = mybir.ActivationFunctionType
ALU = mybir.AluOpType
AX = mybir.AxisListType

NCORE = 8
B = 2
T = 16384
TC = T // NCORE
D = 2048
R = B * TC
DC = D // 128
NEG = -1.0e30
NTOT = float(B * (T + 1) * D)

f32 = lambda a: np.ascontiguousarray(a, dtype=np.float32)
bf16 = lambda a: np.ascontiguousarray(np.asarray(a).astype(ml_dtypes.bfloat16))



class SemCat:

    def __init__(self, nc, es, name):
        self.nc, self.es, self.name = nc, es, name
        self.idx = 0
        self.h = es.enter_context(nc.semaphore(f"{name}0"))
        self.val = 0

    def bump(self, by):
        if self.val + by > 14000:
            self.idx += 1
            self.h = self.es.enter_context(
                self.nc.semaphore(f"{self.name}{self.idx}"))
            self.val = 0
        self.val += by
        return (self.h, self.val)


class Res:
    __slots__ = ("name", "w", "rs")

    def __init__(self, name):
        self.name, self.w, self.rs = name, None, []


ENGS = ("sync", "tensor", "vector", "scalar", "gpsimd")


class TK:

    def __init__(self, nc, es):
        self.nc, self.es = nc, es
        self.sems = {e: SemCat(nc, es, f"c{e[:2]}") for e in ENGS}
        self.dsems = {e: SemCat(nc, es, f"d{e[:2]}") for e in ENGS}
        self.waited = {e: {} for e in ENGS}
        self.tasks = {e: [] for e in ENGS}
        self.ccn = 0
        self.ntasks = 0

    def _collect(self, eng, reads, writes):
        toks = []
        for r in reads:
            if r.w:
                toks.append(r.w)
        for r in writes:
            if r.w:
                toks.append(r.w)
            toks.extend(r.rs)
        m = {}
        for (s, v) in toks:
            k = id(s)
            if v > m.get(k, (None, -1))[1]:
                m[k] = (s, v)
        out = []
        w = self.waited[eng]
        for k, (s, v) in m.items():
            if w.get(k, -1) >= v:
                continue
            w[k] = v
            out.append((s, v))
        return out

    def task(self, eng, fn, reads=(), writes=(), dma=False):
        waits = self._collect(eng, reads, writes)
        cat = self.dsems[eng] if dma else self.sems[eng]
        by = 16 if dma else 1
        tok = cat.bump(by)
        self.tasks[eng].append((waits, fn, tok, by))
        self.ntasks += 1
        for r in reads:
            r.rs.append(tok)
        for r in writes:
            r.w = tok
            r.rs = []
        return tok

    def coll(self, fn, reads=(), writes=()):
        sem = self.es.enter_context(self.nc.semaphore(f"cc{self.ccn}"))
        self.ccn += 1
        waits = self._collect("gpsimd", reads, writes)
        tok = (sem, 1)
        self.tasks["gpsimd"].append((waits, fn, tok, None))
        self.ntasks += 1
        for r in reads:
            r.rs.append(tok)
        for r in writes:
            r.w = tok
            r.rs = []
        return tok

    def flush(self, block):
        for e in ENGS:
            ts = self.tasks[e]
            if not ts:
                continue
            self.tasks[e] = []

            def body(eng, ts=ts):
                for (waits, fn, tok, by) in ts:
                    for (s, v) in waits:
                        eng.wait_ge(s, v)
                    if fn is None:
                        continue
                    inst = fn(eng)
                    if by is None:
                        inst.then_inc(tok[0])
                    else:
                        inst.then_inc(tok[0], by)
            getattr(block, e)(body)

    def barrier(self):
        alltoks = [(c.h, c.val)
                   for c in list(self.sems.values()) + list(self.dsems.values())
                   if c.val > 0]
        for e in ENGS:
            w = self.waited[e]
            waits = []
            for (s, v) in alltoks:
                if w.get(id(s), -1) >= v:
                    continue
                w[id(s)] = v
                waits.append((s, v))
            if waits:
                self.tasks[e].append((waits, None, None, None))

    def final_wait(self):
        alltoks = [(c.h, c.val)
                   for c in list(self.sems.values()) + list(self.dsems.values())
                   if c.val > 0]
        w = self.waited["sync"]
        waits = []
        for (s, v) in alltoks:
            if w.get(id(s), -1) >= v:
                continue
            w[id(s)] = v
            waits.append((s, v))
        self.tasks["sync"].append((waits, None, None, None))


class TB:
    __slots__ = ("t", "r")

    def __init__(self, t, name):
        self.t = t
        self.r = Res(name)

    def __getitem__(self, key):
        return self.t[key]

    @property
    def ap(self):
        return self.t[...] if not hasattr(self.t, "ap") else self.t.ap()



def prep_inputs(inp):
    x = np.asarray(inp["L0_attn_a_hidden_states"], np.float32)

    com = {}

    def colform(v, name):
        com[name] = f32(np.asarray(v, np.float32).reshape(-1, 128).T)

    com["wzT"] = f32(np.asarray(inp["L0_attn_a_W_z"], np.float32).T)
    com["whT"] = f32(np.asarray(inp["L0_attn_a_W_h"], np.float32).T)
    colform(inp["L0_attn_a_b_z"], "bz")
    colform(inp["L0_attn_a_b_h"], "bh")
    colform(inp["L0_attn_a_h0"], "h0")
    com["lnw"] = f32(np.asarray(inp["L0_attn_a_ln_w"], np.float32).reshape(1, D))
    com["lnb"] = f32(np.asarray(inp["L0_attn_a_ln_b"], np.float32).reshape(1, D))

    com["whid"] = f32(inp["L0_attn_b_W_hidden"])
    com["bhid"] = f32(np.asarray(inp["L0_attn_b_b_hidden"], np.float32).reshape(1, 8192))
    colform(np.asarray(inp["L0_attn_b_b_hidden"], np.float32)[4096:], "bhg")
    com["wqkb"] = f32(inp["L0_attn_b_W_qk"])
    com["bqk"] = f32(np.asarray(inp["L0_attn_b_b_qk"], np.float32).reshape(128, 1))
    com["gq"] = f32(np.asarray(inp["L0_attn_b_gamma_q"], np.float32).reshape(128, 1))
    com["bq"] = f32(np.asarray(inp["L0_attn_b_beta_q"], np.float32).reshape(128, 1))
    com["gk"] = f32(np.asarray(inp["L0_attn_b_gamma_k"], np.float32).reshape(128, 1))
    com["bk"] = f32(np.asarray(inp["L0_attn_b_beta_k"], np.float32).reshape(128, 1))
    com["wob"] = f32(inp["L0_attn_b_W_out"])
    com["bob"] = f32(np.asarray(inp["L0_attn_b_b_out"], np.float32).reshape(1, D))

    cw = np.asarray(inp["L0_ffn_conv_w"], np.float32)
    colform(cw[:, 0], "cw0")
    colform(cw[:, 1], "cw1")
    com["wq"] = f32(inp["L0_ffn_Wq"])
    com["wk"] = f32(inp["L0_ffn_Wk"])
    com["wv"] = f32(inp["L0_ffn_Wv"])
    com["wg"] = f32(inp["L0_ffn_Wg"])
    com["wof"] = f32(inp["L0_ffn_Wout"])
    rw = np.asarray(inp["L0_ffn_router_w"], np.float32)
    com["rw0"] = f32(np.broadcast_to(rw[:, :, 0].reshape(1, 2048), (128, 2048)))
    com["rw1"] = f32(np.broadcast_to(rw[:, :, 1].reshape(1, 2048), (128, 2048)))

    jj, ii = np.meshgrid(np.arange(128), np.arange(128), indexing="ij")
    com["tri"] = f32(jj <= ii)
    com["onesq"] = f32(np.ones((128, 128)))
    com["ident"] = f32(np.eye(128))
    com["onesrow"] = f32(np.ones((1, 128)))
    com["onescol"] = f32(np.ones((128, 1)))
    inv = 1.0 / (10000.0 ** (np.arange(0, 64, 2, dtype=np.float64) / 64.0))
    com["cos1"] = f32(np.broadcast_to(np.cos(inv).reshape(1, 32), (128, 32)))
    com["sin1"] = f32(np.broadcast_to(np.sin(inv).reshape(1, 32), (128, 32)))

    in_maps = []
    for c in range(NCORE):
        m = dict(com)
        m["xin"] = f32(x[:, c * TC:(c + 1) * TC, :].reshape(R, D))
        pm = np.zeros((1, 9), np.float32)
        pm[0, :c] = 1.0
        pm[0, 8] = 1.0 if c == 0 else 0.0
        m["pmask"] = pm
        in_maps.append(m)
    return in_maps


COMMON_DECLS = {
    "wzT": ([D, D], F32), "whT": ([D, D], F32),
    "bz": ([128, 16], F32), "bh": ([128, 16], F32), "h0": ([128, 16], F32),
    "lnw": ([1, D], F32), "lnb": ([1, D], F32),
    "whid": ([D, 8192], F32), "bhid": ([1, 8192], F32), "bhg": ([128, 32], F32),
    "wqkb": ([D, 128], F32), "bqk": ([128, 1], F32),
    "gq": ([128, 1], F32), "bq": ([128, 1], F32),
    "gk": ([128, 1], F32), "bk": ([128, 1], F32),
    "wob": ([4096, D], F32), "bob": ([1, D], F32),
    "cw0": ([128, 16], F32), "cw1": ([128, 16], F32),
    "wq": ([D, 1024], F32), "wk": ([D, 1024], F32),
    "wv": ([D, D], F32), "wg": ([D, D], F32), "wof": ([D, D], F32),
    "rw0": ([128, 2048], F32), "rw1": ([128, 2048], F32),
    "tri": ([128, 128], F32), "onesq": ([128, 128], F32),
    "ident": ([128, 128], F32),
    "onesrow": ([1, 128], F32), "onescol": ([128, 1], F32),
    "cos1": ([128, 32], F32), "sin1": ([128, 32], F32),
    "xin": ([R, D], F32), "pmask": ([1, 9], F32),
}



class GK:
    def __init__(self, dev=()):
        self.dev = set(dev)
        self.nc = bass.Bass("TRN2", target_bir_lowering=False)
        self.es = ExitStack()
        self.tk = None
        self.din = {}
        self.dr = {}
        self.out = {}
        self.g = {}

    def declare(self):
        nc = self.nc
        for name, (shape, dt) in COMMON_DECLS.items():
            self.din[name] = TB(nc.dram_tensor(name, shape, dt,
                                               kind="ExternalInput"), name)

        def dram(name, shape, dt):
            self.dr[name] = TB(nc.dram_tensor(name, shape, dt), name)
        dram("X1", [R, D], F32)
        dram("X2", [R, D], F32)
        dram("X3", [R, D], F32)
        dram("Y0", [R, D], BF16)
        dram("Y1", [R, D], BF16)
        dram("Y2", [R, D], BF16)
        dram("Y3", [R, D], BF16)
        dram("LOGH", [R, D], F32)
        dram("Y0LO", [R, D], BF16)
        dram("KLO", [B, DC, 128, TC], BF16)
        dram("WZLO", [D, D], BF16)
        dram("UDBG", [B, DC, 128, TC], F32)
        dram("MDBG", [B, DC, 128, TC], F32)
        dram("CLOC", [B, DC, 128, TC], F32)
        dram("ASTAR", [B, DC, 128, TC], F32)
        dram("XC", [2, DC, 128, TC], BF16)
        dram("GATED", [2, TC, D], BF16)
        dram("TOTIN", [128, 32], F32)
        dram("TOTOUT", [NCORE * 128, 32], F32)
        dram("STATIN", [2, 1], F32)
        dram("STATOUT", [2, 1], F32)
        dram("KLOC", [B, 128, TC], BF16)
        dram("KFULL", [NCORE * B * 128, TC], BF16)
        dram("QLOC", [B, 128, TC], BF16)
        for q in range(4):
            dram(f"VLOC{q}", [B * TC, 1024], BF16)
            dram(f"VFULL{q}", [NCORE * B * TC, 1024], BF16)
        dram("GATEC", [B, 32, 128, TC], BF16)
        dram("GTDC", [B, 32, 128, TC], BF16)

        self.out["out"] = TB(nc.dram_tensor("out", [R, D], F32,
                                            kind="ExternalOutput"), "out")
        for name in self.dev:
            srct = self.dr[name].t
            self.out[name] = TB(nc.dram_tensor("dump_" + name,
                                               list(srct.shape), srct.dtype,
                                               kind="ExternalOutput"), name)

        self.psA = TB(self.es.enter_context(
            nc.psum_tensor("psA", [128, 2048], F32)), "psA")
        self.psB = TB(self.es.enter_context(
            nc.psum_tensor("psB", [128, 2048], F32)), "psB")
        self.bankres = [Res(f"bank{i}") for i in range(8)]
        self.tk = TK(nc, self.es)

    def bank(self, i, cols=512):
        t = self.psA.t if i < 4 else self.psB.t
        off = (i % 4) * 512
        return t[:, off:off + cols], self.bankres[i]

    def banks(self, i0, n):
        assert (i0 % 4) + n <= 4
        t = self.psA.t if i0 < 4 else self.psB.t
        off = (i0 % 4) * 512
        return t[:, off:off + n * 512], self.bankres[i0:i0 + n]

    def sbuf(self, stack, name, shape, dt=F32):
        side = "right" if stack is self.es else "left"
        return TB(stack.enter_context(
            self.nc.sbuf_tensor(name, shape, dt, side=side)), name)

    def bflush(self):
        self.tk.barrier()
        self.tk.flush(self.block)

    def dma(self, q, dst_ap, src_ap, wr, rd=(), cast=False, transpose=False):
        if transpose:
            fn = lambda e, d=dst_ap, s=src_ap: e.dma_start_transpose(d, s)
        else:
            fn = lambda e, d=dst_ap, s=src_ap: e.dma_start(out=d, in_=s)
        if cast:
            assert q == "gpsimd"
        return self.tk.task(q, fn, reads=list(rd), writes=list(wr), dma=True)

    def cbias(self, v):
        if not hasattr(self, "_cpool"):
            self._cpool = self.sbuf(self.es, "g_cpool", [128, 16], F32)
            self._cvals = {}
        if v not in self._cvals:
            j = len(self._cvals)
            assert j < 16
            self._cvals[v] = j
            ap = self._cpool[:, j:j + 1]
            self.tk.task("vector", lambda e, a=ap, v=v: e.memset(a, v),
                         reads=[], writes=[self._cpool.r])
        return self._cpool[:, self._cvals[v]:self._cvals[v] + 1]

    def act(self, out_ap, in_ap, func, wr, rd, bias=0.0, scale=1.0, accum=None):
        if isinstance(bias, float) and bias != 0.0:
            bias = self.cbias(bias)
            rd = list(rd) + [self._cpool.r]
        def fn(e, o=out_ap, i=in_ap, f=func, b=bias, s=scale, a=accum):
            return e.activation(o, i, f, bias=b, scale=s, accum_out=a)
        return self.tk.task("scalar", fn, reads=list(rd), writes=list(wr))

    def dve(self, fn, wr, rd, eng="vector"):
        return self.tk.task(eng, fn, reads=list(rd), writes=list(wr))

    def tt(self, out_ap, a_ap, b_ap, op, wr, rd, eng="vector"):
        def fn(e, o=out_ap, a=a_ap, b=b_ap, op=op):
            return e.tensor_tensor(o, a, b, op)
        return self.tk.task(eng, fn, reads=list(rd), writes=list(wr))

    def ts(self, out_ap, in_ap, s1, s2, op0, op1, wr, rd, eng="vector", accum=None):
        def fn(e, o=out_ap, i=in_ap, s1=s1, s2=s2, op0=op0, op1=op1, a=accum):
            return e.tensor_scalar(o, i, s1, s2, op0, op1, accum_out=a)
        return self.tk.task(eng, fn, reads=list(rd), writes=list(wr))

    def stt(self, out_ap, in0_ap, s, in1_ap, op0, op1, wr, rd, eng="vector"):
        def fn(e, o=out_ap, i0=in0_ap, s=s, i1=in1_ap, op0=op0, op1=op1):
            return e.scalar_tensor_tensor(o, i0, s, i1, op0, op1)
        return self.tk.task(eng, fn, reads=list(rd), writes=list(wr))

    def copy(self, out_ap, in_ap, wr, rd, eng="vector"):
        def fn(e, o=out_ap, i=in_ap):
            return e.tensor_copy(o, i)
        return self.tk.task(eng, fn, reads=list(rd), writes=list(wr))

    def red(self, out_ap, in_ap, op, wr, rd, eng="vector"):
        def fn(e, o=out_ap, i=in_ap, op=op):
            return e.tensor_reduce(o, i, AX.X, op)
        return self.tk.task(eng, fn, reads=list(rd), writes=list(wr))

    def mm(self, steps, wr, rd):
        def fn(e, steps=steps):
            inst = None
            for s in steps:
                if s[0] == "ldw":
                    inst = e.ldweights(s[1])
                elif s[0] == "tp":
                    inst = e.transpose(s[1], s[2], s[3])
                else:
                    o, l, r, st, sp = s
                    inst = e.matmul(o, l, r, start=st, stop=sp)
            return inst
        return self.tk.task("tensor", fn, reads=list(rd), writes=list(wr))

    def rowbc(self, dst_ap, dst_wr, row_ap, row_rd, n, bank_i=7):
        ones = self.g["onesrow"]
        if not hasattr(self, "_rowbuf"):
            self._rowbuf = self.sbuf(self.es, "g_rowbuf", [1, 2048], F32)
        rb = self._rowbuf
        self.dma("sync", rb[0:1, 0:n], row_ap, wr=[rb.r], rd=list(row_rd))
        for c0 in range(0, n, 512):
            w = min(512, n - c0)
            bap, bres = self.bank(bank_i, w)
            self.mm([(bap, ones[0:1, :], rb[0:1, c0:c0 + w], True, True)],
                    wr=[bres], rd=[ones.r, rb.r])
            self.copy(dst_ap[:, c0:c0 + w], bap, wr=list(dst_wr), rd=[bres])

    def lnstats(self, x_ap, x_rd, scr, mu, rstd, tm1, eps=1e-5):
        self.red(mu[:, 0:1].unsqueeze(2), x_ap.unsqueeze(1), ALU.add,
                 wr=[mu.r], rd=list(x_rd))
        self.ts(mu[:, 0:1], mu[:, 0:1], 1.0 / D, None, ALU.mult, ALU.bypass,
                wr=[mu.r], rd=[mu.r])
        self.act(scr[:, 0:x_ap.shape[-1]], x_ap, AF.Square,
                 wr=[scr.r, tm1.r], rd=list(x_rd), accum=tm1[:, 0:1])
        self.tt(rstd[:, 0:1], mu[:, 0:1], mu[:, 0:1], ALU.mult,
                wr=[rstd.r], rd=[mu.r])
        self.stt(rstd[:, 0:1], tm1[:, 0:1], 1.0 / D, rstd[:, 0:1],
                 ALU.mult, ALU.subtract, wr=[rstd.r], rd=[tm1.r, rstd.r])
        self.act(rstd[:, 0:1], rstd[:, 0:1], AF.Ln, wr=[rstd.r], rd=[rstd.r],
                 bias=eps)
        self.act(rstd[:, 0:1], rstd[:, 0:1], AF.Exp, wr=[rstd.r], rd=[rstd.r],
                 scale=-0.5)

    def dump(self, name):
        if name not in self.dev:
            return
        srctb = self.dr[name]
        dst = self.out[name]
        n = srctb.t.shape[0]
        step = max(1, n // 4)
        for i in range(0, n, step):
            sl = slice(i, min(i + step, n))
            self.dma("sync", dst.t[sl], srctb.t[sl],
                     wr=[dst.r], rd=[srctb.r])

    def phase_setup(self):
        gl = self.es
        names = [("bz", [128, 16]), ("bh", [128, 16]), ("h0", [128, 16]),
                 ("cw0", [128, 16]), ("cw1", [128, 16]),
                 ("bqk", [128, 1]), ("gq", [128, 1]), ("bq", [128, 1]),
                 ("gk", [128, 1]), ("bk", [128, 1]), ("bhg", [128, 32]),
                 ("tri", [128, 128]), ("onesq", [128, 128]),
                 ("ident", [128, 128]), ("onesrow", [1, 128]),
                 ("onescol", [128, 1]),
                 ("cos1", [128, 32]), ("sin1", [128, 32]),
                 ("pmask", [1, 9])]
        for n, shape in names:
            tb = self.sbuf(gl, "g_" + n, shape, F32)
            self.g[n] = tb
            self.dma("sync", tb[:, :], self.din[n][:, :], wr=[tb.r],
                     rd=[self.din[n].r])
        nbz = self.sbuf(gl, "g_nbz", [128, 16], F32)
        self.g["nbz"] = nbz
        self.ts(nbz[:, :], self.g["bz"][:, :], -1.0, None, ALU.mult, ALU.bypass,
                wr=[nbz.r], rd=[self.g["bz"].r])
        pmb = self.sbuf(gl, "g_pmb", [128, 9], F32)
        self.g["pmb"] = pmb
        bap, bres = self.bank(7, 9)
        self.mm([(bap, self.g["onesrow"][0:1, :], self.g["pmask"][0:1, :],
                  True, True)], wr=[bres], rd=[self.g["onesrow"].r,
                                               self.g["pmask"].r])
        self.copy(pmb[:, :], bap, wr=[pmb.r], rd=[bres])
        u0 = self.sbuf(gl, "g_u0", [128, 16], F32)
        self.g["u0"] = u0
        s1 = self.sbuf(gl, "g_s1s", [128, 16], F32)
        s2 = self.sbuf(gl, "g_s2s", [128, 16], F32)
        s3 = self.sbuf(gl, "g_s3s", [128, 16], F32)
        mk = self.sbuf(gl, "g_mks", [128, 16], U8)
        h0 = self.g["h0"]
        self.act(s1[:, :], h0[:, :], AF.Relu, wr=[s1.r], rd=[h0.r])
        self.ts(mk[:, :], s1[:, :], 0.0, None, ALU.is_gt, ALU.bypass,
                wr=[mk.r], rd=[s1.r])
        self.act(s1[:, :], s1[:, :], AF.Ln, bias=0.5, wr=[s1.r], rd=[s1.r])
        self.act(s2[:, :], h0[:, :], AF.Sigmoid, wr=[s2.r], rd=[h0.r])
        self.act(s2[:, :], s2[:, :], AF.Ln, wr=[s2.r], rd=[s2.r])
        self.dve(lambda e, o=u0[:, :], m=mk[:, :], a=s1[:, :], b=s2[:, :]:
                 e.select(o, m, a, b), wr=[u0.r], rd=[mk.r, s1.r, s2.r])
        seed = self.sbuf(gl, "g_seed", [128, 16], F32)
        self.g["seed"] = seed
        is0 = pmb[:, 8:9]
        self.ts(s3[:, 0:1], is0, -1.0, NEG * -1.0, ALU.add, ALU.mult,
                wr=[s3.r], rd=[pmb.r])
        self.ts(seed[:, :], u0[:, :], is0, None, ALU.mult, ALU.bypass,
                wr=[seed.r], rd=[u0.r, pmb.r])
        self.ts(seed[:, :], seed[:, :], s3[:, 0:1], None, ALU.add, ALU.bypass,
                wr=[seed.r], rd=[seed.r, s3.r])
        self.g["dots"] = self.sbuf(self.es, "g_dots", [128, 16 * 48], F32)
        u0s = self.sbuf(gl, "g_u0s", [128, 2], F32)
        self.g["u0s"] = u0s
        self.red(u0s[:, 0:1].unsqueeze(2), u0[:, :].unsqueeze(1), ALU.add,
                 wr=[u0s.r], rd=[u0.r])
        self.act(s3[:, :], u0[:, :], AF.Square, accum=u0s[:, 1:2],
                 wr=[s3.r, u0s.r], rd=[u0.r])

    def phase_ln0(self):
        ph = ExitStack()
        xin = self.din["xin"]
        y0 = self.dr["Y0"]
        y0lo = self.dr["Y0LO"]
        xt = [self.sbuf(ph, f"p0_x{i}", [128, D], F32) for i in range(2)]
        yt = [self.sbuf(ph, f"p0_y{i}", [128, D], BF16) for i in range(2)]
        ylt = [self.sbuf(ph, f"p0_yl{i}", [128, D], BF16) for i in range(2)]
        yf = self.sbuf(ph, "p0_yf", [128, D], F32)
        scr = self.sbuf(ph, "p0_scr", [128, D], F32)
        mu = self.sbuf(ph, "p0_mu", [128, 1], F32)
        rstd = self.sbuf(ph, "p0_rs", [128, 1], F32)
        tm1 = self.sbuf(ph, "p0_t1", [128, 1], F32)
        for rb in range(R // 128):
            x = xt[rb % 2]
            y = yt[rb % 2]
            yl = ylt[rb % 2]
            rows = slice(rb * 128, (rb + 1) * 128)
            self.dma("sync", x[:, :], xin.t[rows, :], wr=[x.r], rd=[xin.r])
            self.lnstats(x[:, :], [x.r], scr, mu, rstd, tm1)
            self.ts(yf[:, :], x[:, :], mu[:, 0:1], rstd[:, 0:1],
                    ALU.subtract, ALU.mult, wr=[yf.r], rd=[x.r, mu.r, rstd.r])
            self.copy(y[:, :], yf[:, :], wr=[y.r], rd=[yf.r])
            self.dma("sync", y0.t[rows, :], y[:, :], wr=[y0.r], rd=[y.r])
            self.tt(scr[:, :], yf[:, :], y[:, :], ALU.subtract,
                    wr=[scr.r], rd=[yf.r, y.r])
            self.copy(yl[:, :], scr[:, :], wr=[yl.r], rd=[scr.r])
            self.dma("sync", y0lo.t[rows, :], yl[:, :], wr=[y0lo.r], rd=[yl.r])
        return ph

    def phase_wsplit(self):
        ph = ExitStack()
        wzT = self.din["wzT"]
        wlo = self.dr["WZLO"]
        wt = [self.sbuf(ph, f"pw_w{i}", [128, D], F32) for i in range(2)]
        hb = self.sbuf(ph, "pw_h", [128, D], BF16)
        hf = self.sbuf(ph, "pw_hf", [128, D], F32)
        lb = [self.sbuf(ph, f"pw_l{i}", [128, D], BF16) for i in range(2)]
        for rt in range(DC):
            w = wt[rt % 2]
            l = lb[rt % 2]
            rows = slice(rt * 128, (rt + 1) * 128)
            self.dma("sync", w[:, :], wzT.t[rows, :], wr=[w.r], rd=[wzT.r])
            self.copy(hb[:, :], w[:, :], wr=[hb.r], rd=[w.r])
            self.copy(hf[:, :], hb[:, :], wr=[hf.r], rd=[hb.r])
            self.tt(hf[:, :], w[:, :], hf[:, :], ALU.subtract,
                    wr=[hf.r], rd=[w.r, hf.r])
            self.copy(l[:, :], hf[:, :], wr=[l.r], rd=[hf.r])
            self.dma("sync", wlo.t[rows, :], l[:, :], wr=[wlo.r], rd=[l.r])
        return ph

    def phase_a1pre(self):
        ph = ExitStack()
        y0lo = self.dr["Y0LO"]
        klo = self.dr["KLO"]
        ylofm = self.sbuf(ph, "ap_ylo", [128, DC * TC], BF16)
        wcol = [self.sbuf(ph, f"ap_w{i}", [128, D], BF16) for i in range(2)]
        kb = [self.sbuf(ph, f"ap_k{i}", [128, TC], BF16) for i in range(2)]
        for b in range(B):
            for dc in range(DC):
                self.dma("sync", ylofm[:, dc * TC:(dc + 1) * TC],
                         y0lo.t[b * TC:(b + 1) * TC, dc * 128:(dc + 1) * 128],
                         wr=[ylofm.r], rd=[y0lo.r], transpose=True)
            for hc in range(DC):
                w = wcol[hc % 2]
                self.dma("gpsimd",
                         w[:, :].rearrange("p (kc i) -> p kc i", kc=DC),
                         self.din["wzT"].t[:, hc * 128:(hc + 1) * 128]
                         .rearrange("(kc p) i -> p kc i", p=128),
                         wr=[w.r], rd=[self.din["wzT"].r], cast=True)
                bq = 4 * (hc % 2)
                steps = []
                for kc in range(DC):
                    steps.append(("ldw", w[:, kc * 128:(kc + 1) * 128]))
                    for n in range(4):
                        bap, _ = self.bank(bq + n)
                        steps.append((bap, w[:, kc * 128:(kc + 1) * 128],
                                      ylofm[:, kc * TC + n * 512:
                                           kc * TC + (n + 1) * 512],
                                      kc == 0, kc == DC - 1))
                self.mm(steps, wr=self.bankres[bq:bq + 4], rd=[w.r, ylofm.r])
                pap, pres = self.banks(bq, 4)
                kk = kb[hc % 2]
                self.copy(kk[:, :], pap, wr=[kk.r], rd=pres)
                self.dma("sync", klo.t[b, hc], kk[:, :], wr=[klo.r], rd=[kk.r])
        return ph

    def phase_a1(self):
        ph = ExitStack()
        g = self.g
        y0 = self.dr["Y0"]
        cloc, astar = self.dr["CLOC"], self.dr["ASTAR"]
        totin = self.dr["TOTIN"]

        y0fm = self.sbuf(ph, "a1_y0fm", [128, DC * TC], BF16)
        wcol = [self.sbuf(ph, f"a1_w{i}", [128, 3 * D], BF16) for i in range(2)]
        klot = [self.sbuf(ph, f"a1_kl{i}", [128, TC], BF16) for i in range(2)]
        s0 = self.sbuf(ph, "a1_s0", [128, TC], F32)
        klo = self.dr["KLO"]
        s1 = self.sbuf(ph, "a1_s1", [128, TC], F32)
        s2 = self.sbuf(ph, "a1_s2", [128, TC], F32)
        b1b = self.sbuf(ph, "a1_b1", [128, TC], F32)
        b2b = self.sbuf(ph, "a1_b2", [128, TC], F32)
        mkb = self.sbuf(ph, "a1_mk", [128, TC], U8)
        lcb = [self.sbuf(ph, f"a1_lc{i}", [128, TC], F32) for i in range(2)]
        ub = self.sbuf(ph, "a1_u", [128, TC], F32)
        cb = self.sbuf(ph, "a1_c", [128, TC], F32)
        asb = self.sbuf(ph, "a1_as", [128, TC], F32)
        lcsum = self.sbuf(ph, "a1_lcs", [128, TC], F32)
        tot = self.sbuf(ph, "a1_tot", [128, 32], F32)

        for b in range(B):
            for dc in range(DC):
                self.dma("sync", y0fm[:, dc * TC:(dc + 1) * TC],
                         y0.t[b * TC:(b + 1) * TC, dc * 128:(dc + 1) * 128],
                         wr=[y0fm.r], rd=[y0.r], transpose=True)
            for hc in range(DC):
                w = wcol[hc % 2]
                lc = lcb[hc % 2]
                self.dma("gpsimd", w[:, 0:D].rearrange("p (kc i) -> p kc i", kc=DC),
                         self.din["wzT"].t[:, hc * 128:(hc + 1) * 128]
                         .rearrange("(kc p) i -> p kc i", p=128),
                         wr=[w.r], rd=[self.din["wzT"].r], cast=True)
                self.dma("gpsimd", w[:, D:2 * D].rearrange("p (kc i) -> p kc i", kc=DC),
                         self.din["whT"].t[:, hc * 128:(hc + 1) * 128]
                         .rearrange("(kc p) i -> p kc i", p=128),
                         wr=[w.r], rd=[self.din["whT"].r], cast=True)
                self.dma("sync", w[:, 2 * D:3 * D].rearrange("p (kc i) -> p kc i", kc=DC),
                         self.dr["WZLO"].t[:, hc * 128:(hc + 1) * 128]
                         .rearrange("(kc p) i -> p kc i", p=128),
                         wr=[w.r], rd=[self.dr["WZLO"].r])
                kl = klot[hc % 2]
                self.dma("sync", kl[:, :], klo.t[b, hc], wr=[kl.r], rd=[klo.r])
                steps = []
                for kc in range(DC):
                    for w0 in (0, 2 * D):
                        steps.append(("ldw", w[:, w0 + kc * 128:
                                             w0 + (kc + 1) * 128]))
                        for n in range(4):
                            bap, _ = self.bank(n)
                            steps.append((bap, w[:, w0 + kc * 128:
                                              w0 + (kc + 1) * 128],
                                          y0fm[:, kc * TC + n * 512:
                                               kc * TC + (n + 1) * 512],
                                          kc == 0 and w0 == 0,
                                          kc == DC - 1 and w0 > 0))
                kap, kres = self.banks(0, 4)
                self.mm(steps, wr=self.bankres[0:4], rd=[w.r, y0fm.r])
                self.tt(s0[:, :], kap, kl[:, :], ALU.add,
                        wr=[s0.r], rd=kres + [kl.r])
                self.act(s1[:, :], s0[:, :], AF.Sigmoid, bias=g["bz"][:, hc:hc + 1],
                         wr=[s1.r], rd=[s0.r, g["bz"].r])
                self.act(s2[:, :], s0[:, :], AF.Sigmoid, scale=-1.0,
                         bias=g["nbz"][:, hc:hc + 1],
                         wr=[s2.r], rd=[s0.r, g["nbz"].r])
                self.act(ub[:, :], s1[:, :], AF.Ln, wr=[ub.r], rd=[s1.r])
                self.act(lc[:, :], s2[:, :], AF.Ln, wr=[lc.r], rd=[s2.r])
                steps = []
                for kc in range(DC):
                    steps.append(("ldw", w[:, D + kc * 128:D + (kc + 1) * 128]))
                    for n in range(4):
                        bap, _ = self.bank(4 + n)
                        steps.append((bap, w[:, D + kc * 128:D + (kc + 1) * 128],
                                      y0fm[:, kc * TC + n * 512:
                                           kc * TC + (n + 1) * 512],
                                      kc == 0, kc == DC - 1))
                tap, tres = self.banks(4, 4)
                self.mm(steps, wr=self.bankres[4:8], rd=[w.r, y0fm.r])
                self.act(s1[:, :], tap, AF.Relu, bias=g["bh"][:, hc:hc + 1],
                         wr=[s1.r], rd=tres + [g["bh"].r])
                self.act(s2[:, :], tap, AF.Sigmoid, bias=g["bh"][:, hc:hc + 1],
                         wr=[s2.r], rd=tres + [g["bh"].r])
                self.ts(mkb[:, :], s1[:, :], 0.0, None, ALU.is_gt, ALU.bypass,
                        wr=[mkb.r], rd=[s1.r])
                self.act(b1b[:, :], s1[:, :], AF.Ln, bias=0.5,
                         wr=[b1b.r], rd=[s1.r])
                self.act(b2b[:, :], s2[:, :], AF.Ln, wr=[b2b.r], rd=[s2.r])
                self.dve(lambda e, o=s1[:, :], m=mkb[:, :], a=b1b[:, :],
                         b_=b2b[:, :]: e.select(o, m, a, b_),
                         wr=[s1.r], rd=[mkb.r, b1b.r, b2b.r])
                self.tt(ub[:, :], ub[:, :], s1[:, :], ALU.add,
                        wr=[ub.r], rd=[ub.r, s1.r])
                steps = []
                for n in range(4):
                    bap, _ = self.bank(n)
                    tsl = slice(n * 512, (n + 1) * 512)
                    if hc > 0:
                        steps.append((bap, g["onesq"][:, :], lcsum[:, tsl],
                                      True, False))
                        steps.append((bap, g["tri"][:, :], lc[:, tsl],
                                      False, True))
                    else:
                        steps.append((bap, g["tri"][:, :], lc[:, tsl],
                                      True, True))
                rd = [g["tri"].r, lc.r] + ([g["onesq"].r, lcsum.r] if hc > 0 else [])
                aap, ares = self.banks(0, 4)
                self.mm(steps, wr=self.bankres[0:4], rd=rd)
                self.copy(asb[:, :], aap, wr=[asb.r], rd=ares)
                self.dma("sync", astar.t[b, hc], asb[:, :],
                         wr=[astar.r], rd=[asb.r])
                self.tt(ub[:, :], ub[:, :], asb[:, :], ALU.subtract,
                        wr=[ub.r], rd=[ub.r, asb.r])
                if hc == 0:
                    self.copy(lcsum[:, :], lc[:, :], wr=[lcsum.r], rd=[lc.r])
                else:
                    self.tt(lcsum[:, :], lcsum[:, :], lc[:, :], ALU.add,
                            wr=[lcsum.r], rd=[lcsum.r, lc.r])
                pap, prd = g["seed"][:, hc:hc + 1], [g["seed"].r]
                if "UDBG" in self.dev:
                    self.dma("sync", self.dr["UDBG"].t[b, hc], ub[:, :],
                             wr=[self.dr["UDBG"].r], rd=[ub.r])
                self.dve(lambda e, o=s1[:, :], d=ub[:, :], i=pap:
                         e.tensor_tensor_scan(o, d, d, i, ALU.max, ALU.bypass),
                         wr=[s1.r], rd=[ub.r] + prd)
                if "MDBG" in self.dev:
                    self.dma("sync", self.dr["MDBG"].t[b, hc], s1[:, :],
                             wr=[self.dr["MDBG"].r], rd=[s1.r])
                self.tt(s2[:, 1:TC], s1[:, 0:TC - 1], s1[:, 1:TC],
                        ALU.subtract, wr=[s2.r], rd=[s1.r])
                self.tt(s2[:, 0:1], pap, s1[:, 0:1], ALU.subtract,
                        wr=[s2.r], rd=[s1.r] + prd)
                self.act(s2[:, :], s2[:, :], AF.Exp, wr=[s2.r], rd=[s2.r])
                self.tt(b1b[:, :], ub[:, :], s1[:, :], ALU.subtract,
                        wr=[b1b.r], rd=[ub.r, s1.r])
                self.act(b1b[:, :], b1b[:, :], AF.Exp, wr=[b1b.r], rd=[b1b.r])
                self.dve(lambda e, o=b2b[:, :], a=s2[:, :], b_=b1b[:, :]:
                         e.tensor_tensor_scan(o, a, b_, 1.0, ALU.mult, ALU.add),
                         wr=[b2b.r], rd=[s2.r, b1b.r])
                self.act(b2b[:, :], b2b[:, :], AF.Ln, wr=[b2b.r], rd=[b2b.r])
                self.tt(cb[:, :], s1[:, :], b2b[:, :], ALU.add,
                        wr=[cb.r], rd=[s1.r, b2b.r])
                self.copy(tot[:, b * 16 + hc:b * 16 + hc + 1],
                          cb[:, TC - 1:TC], wr=[tot.r], rd=[cb.r])
                self.dma("sync", cloc.t[b, hc], cb[:, :], wr=[cloc.r], rd=[cb.r])
        self.dma("sync", totin.t[:, :], tot[:, :], wr=[totin.r], rd=[tot.r])
        tout = self.dr["TOTOUT"]
        self.tk.coll(lambda e, i=totin.t[...], o=tout.t[...]:
                     e.collective_compute("AllGather", ALU.bypass,
                                          replica_groups=[list(range(NCORE))],
                                          ins=[i], outs=[o]),
                     reads=[totin.r], writes=[tout.r])
        return ph

    def phase_a2(self):
        ph = ExitStack()
        g = self.g
        cloc, astar, logh = self.dr["CLOC"], self.dr["ASTAR"], self.dr["LOGH"]
        tout = self.dr["TOTOUT"]

        tot2 = self.sbuf(ph, "a2_tot", [128, 256], F32)
        mxb = self.sbuf(ph, "a2_mx", [128, 32], F32)
        scc = self.sbuf(ph, "a2_scc", [128, 256], F32)
        pall = self.sbuf(ph, "a2_p", [128, 32], F32)
        clb = [self.sbuf(ph, f"a2_cl{i}", [128, TC], F32) for i in range(2)]
        asb = [self.sbuf(ph, f"a2_as{i}", [128, TC], F32) for i in range(2)]
        sc1 = self.sbuf(ph, "a2_s1", [128, TC], F32)
        sc2 = self.sbuf(ph, "a2_s2", [128, TC], F32)
        sc3 = self.sbuf(ph, "a2_s3", [128, TC], F32)
        lhb = self.sbuf(ph, "a2_lh", [128, TC], F32)
        sqs = self.sbuf(ph, "a2_sq", [128, TC], F32)
        tst = self.sbuf(ph, "a2_tst", [128, 512], F32)
        runs = self.sbuf(ph, "a2_run", [128, 2], F32)
        st1 = self.sbuf(ph, "a2_st1", [128, 2], F32)
        sums = self.sbuf(ph, "a2_sum", [1, 4], F32)

        self.dma("sync", tot2[:, :].rearrange("p (r c) -> p r c", r=NCORE),
                 tout.t[...].rearrange("(r p) c -> p r c", p=128),
                 wr=[tot2.r], rd=[tout.r])
        t3 = tot2[:, :].rearrange("p (r c) -> p r c", r=NCORE)
        maskbc = g["pmb"][:, 0:8].unsqueeze(2).broadcast_to([128, 8, 32])
        self.tt(t3, t3, maskbc, ALU.mult, wr=[tot2.r], rd=[tot2.r, g["pmb"].r])
        self.ts(scc[:, 0:8], g["pmb"][:, 0:8], -1.0, 1e30, ALU.add, ALU.mult,
                wr=[scc.r], rd=[g["pmb"].r])
        self.tt(t3, t3, scc[:, 0:8].unsqueeze(2).broadcast_to([128, 8, 32]),
                ALU.add, wr=[tot2.r], rd=[tot2.r, scc.r])
        tcr = tot2[:, :].rearrange("p (r c) -> p c r", r=NCORE)
        self.red(mxb[:, :].unsqueeze(2), tcr, ALU.max, wr=[mxb.r], rd=[tot2.r])
        sccv = scc[:, :].rearrange("p (c r) -> p c r", r=NCORE)
        self.tt(sccv, tcr, mxb[:, :].unsqueeze(2).broadcast_to([128, 32, 8]),
                ALU.subtract, wr=[scc.r], rd=[tot2.r, mxb.r])
        self.act(scc[:, :], scc[:, :], AF.Exp, wr=[scc.r], rd=[scc.r])
        self.red(pall[:, :].unsqueeze(2), sccv, ALU.add, wr=[pall.r], rd=[scc.r])
        self.act(pall[:, :], pall[:, :], AF.Ln, wr=[pall.r], rd=[pall.r])
        self.tt(pall[:, :], pall[:, :], mxb[:, :], ALU.add,
                wr=[pall.r], rd=[pall.r, mxb.r])

        self.dve(lambda e, o=runs[:, :]: e.memset(o, 0.0), wr=[runs.r], rd=[])

        for b in range(B):
            for hc in range(DC):
                cl = clb[hc % 2]
                a_ = asb[hc % 2]
                self.dma("sync", cl[:, :], cloc.t[b, hc], wr=[cl.r],
                         rd=[cloc.r])
                self.dma("sync", a_[:, :], astar.t[b, hc], wr=[a_.r],
                         rd=[astar.r])
                pcol = pall[:, b * 16 + hc:b * 16 + hc + 1]
                self.ts(sc1[:, :], cl[:, :], pcol, None, ALU.subtract,
                        ALU.bypass, wr=[sc1.r], rd=[cl.r, pall.r])
                self.act(sc2[:, :], sc1[:, :], AF.Abs, wr=[sc2.r], rd=[sc1.r])
                self.act(sc3[:, :], sc2[:, :], AF.Exp, scale=-1.0,
                         wr=[sc3.r], rd=[sc2.r])
                self.act(sc1[:, :], sc3[:, :], AF.Ln, bias=1.0,
                         wr=[sc1.r], rd=[sc3.r])
                self.ts(sc2[:, :], cl[:, :], pcol, None, ALU.max, ALU.bypass,
                        wr=[sc2.r], rd=[cl.r, pall.r])
                self.tt(lhb[:, :], sc1[:, :], sc2[:, :], ALU.add,
                        wr=[lhb.r], rd=[sc1.r, sc2.r])
                self.tt(lhb[:, :], lhb[:, :], a_[:, :], ALU.add,
                        wr=[lhb.r], rd=[lhb.r, a_.r])
                self.red(st1[:, 0:1].unsqueeze(2), lhb[:, :].unsqueeze(1),
                         ALU.add, wr=[st1.r], rd=[lhb.r])
                self.act(sqs[:, :], lhb[:, :], AF.Square, accum=st1[:, 1:2],
                         wr=[sqs.r, st1.r], rd=[lhb.r])
                self.tt(runs[:, :], runs[:, :], st1[:, :], ALU.add,
                        wr=[runs.r], rd=[runs.r, st1.r])
                for grp in range(4):
                    bi = grp
                    bap, bres = self.bank(bi)
                    steps = []
                    for s in range(4):
                        tb = grp * 4 + s
                        steps.append(("tp", bap[:, s * 128:(s + 1) * 128],
                                      lhb[:, tb * 128:(tb + 1) * 128],
                                      g["ident"][:, :]))
                    self.mm(steps, wr=[bres], rd=[lhb.r, g["ident"].r])
                    self.copy(tst[:, :], bap, wr=[tst.r], rd=[bres])
                    for s in range(4):
                        tb = grp * 4 + s
                        rows = slice(b * TC + tb * 128, b * TC + (tb + 1) * 128)
                        self.dma("sync", logh.t[rows, hc * 128:(hc + 1) * 128],
                                 tst[:, s * 128:(s + 1) * 128],
                                 wr=[logh.r], rd=[tst.r])

        self.ts(st1[:, :], g["u0s"][:, :], 0.25, None, ALU.mult, ALU.bypass,
                wr=[st1.r], rd=[g["u0s"].r])
        self.tt(runs[:, :], runs[:, :], st1[:, :], ALU.add,
                wr=[runs.r], rd=[runs.r, st1.r])
        bap, bres = self.bank(7, 2)
        self.mm([(bap[0:2, 0:1], runs[:, :], g["onescol"][:, :], True, True)],
                wr=[bres], rd=[runs.r, g["onescol"].r])
        stin, stout = self.dr["STATIN"], self.dr["STATOUT"]
        self.copy(st1[0:2, 0:1], bap[0:2, 0:1], wr=[st1.r], rd=[bres])
        self.dma("sync", stin.t[...], st1[0:2, 0:1], wr=[stin.r], rd=[st1.r])
        self.tk.coll(lambda e, i=stin.t[...], o=stout.t[...]:
                     e.collective_compute("AllReduce", ALU.add,
                                          replica_groups=[list(range(NCORE))],
                                          ins=[i], outs=[o]),
                     reads=[stin.r], writes=[stout.r])
        self.dma("sync", sums[0:1, 0:2],
                 stout.t[...].rearrange("a b -> b a"), wr=[sums.r],
                 rd=[stout.r])
        S, Q = sums[0:1, 0:1], sums[0:1, 1:2]
        mean, var, t1 = sums[0:1, 2:3], sums[0:1, 3:4], st1[0:1, 0:1]
        self.ts(mean, S, 1.0 / NTOT, None, ALU.mult, ALU.bypass,
                wr=[sums.r], rd=[sums.r])
        self.tt(t1, mean, S, ALU.mult, wr=[st1.r], rd=[sums.r])
        self.tt(var, Q, t1, ALU.subtract, wr=[sums.r], rd=[sums.r, st1.r])
        self.ts(var, var, 1.0 / (NTOT - 1.0), None, ALU.mult, ALU.bypass,
                wr=[sums.r], rd=[sums.r])
        self.act(var, var, AF.Ln, wr=[sums.r], rd=[sums.r])
        self.act(var, var, AF.Exp, scale=-0.5, wr=[sums.r], rd=[sums.r])
        self.copy(S, var, wr=[sums.r], rd=[sums.r])
        self.tt(Q, mean, var, ALU.mult, wr=[sums.r], rd=[sums.r])
        self.ts(Q, Q, -1.0, None, ALU.mult, ALU.bypass, wr=[sums.r], rd=[sums.r])
        stbc = self.sbuf(self.es, "g_stbc", [128, 2], F32)
        self.g["stbc"] = stbc
        bap2, bres2 = self.bank(6, 2)
        self.mm([(bap2, g["onesrow"][0:1, :], sums[0:1, 0:2], True, True)],
                wr=[bres2], rd=[g["onesrow"].r, sums.r])
        self.copy(stbc[:, :], bap2, wr=[stbc.r], rd=[bres2])
        return ph

    def phase_a3(self):
        ph = ExitStack()
        g = self.g
        logh, x1d, y1d = self.dr["LOGH"], self.dr["X1"], self.dr["Y1"]
        xin = self.din["xin"]

        lnwbc = self.sbuf(ph, "a3_lnw", [128, D], F32)
        lnbbc = self.sbuf(ph, "a3_lnb", [128, D], F32)
        self.rowbc(lnwbc[:, :], [lnwbc.r], self.din["lnw"].t[...],
                   [self.din["lnw"].r], D)
        self.rowbc(lnbbc[:, :], [lnbbc.r], self.din["lnb"].t[...],
                   [self.din["lnb"].r], D)

        lht = [self.sbuf(ph, f"a3_lh{i}", [128, D], F32) for i in range(2)]
        xt = [self.sbuf(ph, f"a3_x{i}", [128, D], F32) for i in range(2)]
        hb = self.sbuf(ph, "a3_h", [128, D], F32)
        y0b = self.sbuf(ph, "a3_y0", [128, D], F32)
        tb2 = self.sbuf(ph, "a3_t2", [128, D], F32)
        scr1 = self.sbuf(ph, "a3_s1", [128, D], F32)
        scr2 = self.sbuf(ph, "a3_s2", [128, D], F32)
        x1b = [self.sbuf(ph, f"a3_x1{i}", [128, D], F32) for i in range(2)]
        y1b = [self.sbuf(ph, f"a3_y1{i}", [128, D], BF16) for i in range(2)]
        mu = self.sbuf(ph, "a3_mu", [128, 1], F32)
        rstd = self.sbuf(ph, "a3_rs", [128, 1], F32)
        tm1 = self.sbuf(ph, "a3_t1", [128, 1], F32)
        stbc = self.g["stbc"]

        for rb in range(R // 128):
            rows = slice(rb * 128, (rb + 1) * 128)
            lh = lht[rb % 2]
            x = xt[rb % 2]
            x1 = x1b[rb % 2]
            y1 = y1b[rb % 2]
            self.dma("sync", lh[:, :], logh.t[rows, :], wr=[lh.r], rd=[logh.r])
            self.dma("sync", x[:, :], xin.t[rows, :], wr=[x.r], rd=[xin.r])
            self.ts(scr1[:, :], lh[:, :], stbc[:, 0:1], stbc[:, 1:2],
                    ALU.mult, ALU.add, wr=[scr1.r], rd=[lh.r, stbc.r])
            self.act(hb[:, :], scr1[:, :], AF.Exp, wr=[hb.r], rd=[scr1.r])
            self.lnstats(x[:, :], [x.r], scr2, mu, rstd, tm1)
            self.ts(y0b[:, :], x[:, :], mu[:, 0:1], rstd[:, 0:1],
                    ALU.subtract, ALU.mult, wr=[y0b.r], rd=[x.r, mu.r, rstd.r])
            self.tt(tb2[:, :], hb[:, :], y0b[:, :], ALU.add,
                    wr=[tb2.r], rd=[hb.r, y0b.r], eng="gpsimd")
            self.lnstats(tb2[:, :], [tb2.r], scr2, mu, rstd, tm1)
            self.ts(scr1[:, :], tb2[:, :], mu[:, 0:1], rstd[:, 0:1],
                    ALU.subtract, ALU.mult, wr=[scr1.r],
                    rd=[tb2.r, mu.r, rstd.r])
            self.tt(scr2[:, :], scr1[:, :], lnwbc[:, :], ALU.mult,
                    wr=[scr2.r], rd=[scr1.r, lnwbc.r], eng="gpsimd")
            self.tt(scr2[:, :], scr2[:, :], lnbbc[:, :], ALU.add,
                    wr=[scr2.r], rd=[scr2.r, lnbbc.r])
            self.tt(x1[:, :], scr2[:, :], x[:, :], ALU.add,
                    wr=[x1.r], rd=[scr2.r, x.r], eng="gpsimd")
            self.dma("sync", x1d.t[rows, :], x1[:, :], wr=[x1d.r], rd=[x1.r])
            self.lnstats(x1[:, :], [x1.r], scr2, mu, rstd, tm1)
            self.ts(y1[:, :], x1[:, :], mu[:, 0:1], rstd[:, 0:1],
                    ALU.subtract, ALU.mult, wr=[y1.r], rd=[x1.r, mu.r, rstd.r])
            self.dma("sync", y1d.t[rows, :], y1[:, :], wr=[y1d.r], rd=[y1.r])
        return ph

    def phase_ffn(self, yin, xinN, xoutN, youtN, tag=""):
        g = self.g
        ydr = self.dr[yin]
        xdr = self.dr[xinN] if xinN in self.dr else self.din[xinN]
        xodr = self.out["out"] if xoutN == "out" else self.dr[xoutN]
        yodr = self.dr[youtN] if youtN else None
        xc, gated = self.dr["XC"], self.dr["GATED"]

        ph = ExitStack()
        y0c = self.sbuf(ph, tag + "fc_y0", [128, TC], BF16)
        y1c = self.sbuf(ph, tag + "fc_y1", [128, TC], BF16)
        t0 = self.sbuf(ph, tag + "fc_t0", [128, TC], F32)
        t1 = self.sbuf(ph, tag + "fc_t1", [128, TC], F32)
        t2 = self.sbuf(ph, tag + "fc_t2", [128, TC], F32)
        xcb = [self.sbuf(ph, tag + f"fc_xc{i}", [128, TC], BF16) for i in range(2)]
        for dc in range(DC):
            cols = slice(dc * 128, (dc + 1) * 128)
            self.dma("sync", y0c[:, :], ydr.t[0:TC, cols], wr=[y0c.r],
                     rd=[ydr.r], transpose=True)
            self.dma("sync", y1c[:, :], ydr.t[TC:2 * TC, cols], wr=[y1c.r],
                     rd=[ydr.r], transpose=True)
            self.ts(t0[:, :], y0c[:, :], g["cw1"][:, dc:dc + 1], None,
                    ALU.mult, ALU.bypass, wr=[t0.r], rd=[y0c.r, g["cw1"].r])
            self.act(xcb[0][:, :], t0[:, :], AF.Silu, wr=[xcb[0].r], rd=[t0.r])
            self.dma("sync", xc.t[0, dc], xcb[0][:, :], wr=[xc.r], rd=[xcb[0].r])
            self.ts(t1[:, :], y0c[:, :], g["cw0"][:, dc:dc + 1], None,
                    ALU.mult, ALU.bypass, wr=[t1.r], rd=[y0c.r, g["cw0"].r])
            self.ts(t2[:, :], y1c[:, :], g["cw1"][:, dc:dc + 1], None,
                    ALU.mult, ALU.bypass, wr=[t2.r], rd=[y1c.r, g["cw1"].r])
            self.tt(t1[:, :], t1[:, :], t2[:, :], ALU.add, wr=[t1.r],
                    rd=[t1.r, t2.r])
            self.act(xcb[1][:, :], t1[:, :], AF.Silu, wr=[xcb[1].r], rd=[t1.r])
            self.dma("sync", xc.t[1, dc], xcb[1][:, :], wr=[xc.r], rd=[xcb[1].r])
        self.bflush()
        ph.close()

        ph = ExitStack()
        wqk = self.sbuf(ph, tag + "f1_w", [128, DC * 2048], BF16)
        self.dma("gpsimd", wqk[:, :].rearrange("p (kc z) -> p kc z", z=2048)
                 [:, :, 0:1024],
                 self.din["wq"].t[...].rearrange("(kc p) w -> p kc w", p=128),
                 wr=[wqk.r], rd=[self.din["wq"].r], cast=True)
        self.dma("gpsimd", wqk[:, :].rearrange("p (kc z) -> p kc z", z=2048)
                 [:, :, 1024:2048],
                 self.din["wk"].t[...].rearrange("(kc p) w -> p kc w", p=128),
                 wr=[wqk.r], rd=[self.din["wk"].r], cast=True)
        xcs = [self.sbuf(ph, tag + f"f1_xc{i}", [128, DC * 128], BF16)
               for i in range(4)]
        qb = [self.sbuf(ph, tag + f"f1_q{i}", [128, 1024], F32) for i in range(2)]
        kb = [self.sbuf(ph, tag + f"f1_k{i}", [128, 1024], F32) for i in range(2)]
        qr = self.sbuf(ph, tag + "f1_qr", [128, 1024], F32)
        kr = self.sbuf(ph, tag + "f1_kr", [128, 1024], F32)
        rsc = self.sbuf(ph, tag + "f1_rs", [128, 1024], F32)
        msc = self.sbuf(ph, tag + "f1_ms", [128, 1024], F32)
        dots = self.g["dots"]
        cosv = g["cos1"][:, :].unsqueeze(1).broadcast_to([128, 16, 32])
        sinv = g["sin1"][:, :].unsqueeze(1).broadcast_to([128, 16, 32])

        def rot(dst, dstr, src, srcr):
            sv = src.rearrange("p (h two s) -> p h two s", two=2, s=32)
            dv = dst.rearrange("p (h two s) -> p h two s", two=2, s=32)
            a, bb = sv[:, :, 0, :], sv[:, :, 1, :]
            da, db_ = dv[:, :, 0, :], dv[:, :, 1, :]
            mv = msc[:, 0:512].rearrange("p (h s) -> p h s", s=32)
            self.tt(da, a, cosv, ALU.mult, wr=[dstr], rd=srcr + [g["cos1"].r])
            self.tt(mv, bb, sinv, ALU.mult, wr=[msc.r],
                    rd=srcr + [g["sin1"].r])
            self.tt(da, da, mv, ALU.subtract, wr=[dstr], rd=[dstr, msc.r])
            self.tt(db_, bb, cosv, ALU.mult, wr=[dstr], rd=srcr + [g["cos1"].r])
            self.tt(mv, a, sinv, ALU.mult, wr=[msc.r],
                    rd=srcr + [g["sin1"].r])
            self.tt(db_, db_, mv, ALU.add, wr=[dstr], rd=[dstr, msc.r])

        for tc in range(16):
            for l in range(2):
                x_ = xcs[(tc * 2 + l) % 4]
                self.dma("sync", x_[:, :].rearrange("p (kc t) -> p kc t", t=128),
                         xc.t[l][:, :, tc * 128:(tc + 1) * 128]
                         .rearrange("kc p t -> p kc t"),
                         wr=[x_.r], rd=[xc.r])
                steps = []
                for kc in range(DC):
                    steps.append(("ldw", x_[:, kc * 128:(kc + 1) * 128]))
                    for n in range(4):
                        bap, _ = self.bank(l * 4 + n)
                        steps.append((bap, x_[:, kc * 128:(kc + 1) * 128],
                                      wqk[:, kc * 2048 + n * 512:
                                          kc * 2048 + (n + 1) * 512],
                                      kc == 0, kc == DC - 1))
                self.mm(steps, wr=self.bankres[l * 4:l * 4 + 4],
                        rd=[x_.r, wqk.r])
                pq, pqr = self.banks(l * 4, 2)
                pk, pkr = self.banks(l * 4 + 2, 2)
                self.act(qb[l][:, :], pq, AF.Silu, wr=[qb[l].r], rd=pqr)
                self.copy(kb[l][:, :], pk, wr=[kb[l].r], rd=pkr)
            rot(qr[:, :], qr.r, qb[1][:, :], [qb[1].r])
            rot(kr[:, :], kr.r, kb[1][:, :], [kb[1].r])
            base = tc * 48
            for di, (qa, qa_r, ka, ka_r) in enumerate(
                    [(qb[0][:, :], qb[0].r, kb[0][:, :], kb[0].r),
                     (qr[:, :], qr.r, kb[0][:, :], kb[0].r),
                     (qr[:, :], qr.r, kr[:, :], kr.r)]):
                self.tt(msc[:, :], qa, ka, ALU.mult, wr=[msc.r],
                        rd=[qa_r, ka_r])
                self.red(dots[:, base + di * 16:base + (di + 1) * 16]
                         .unsqueeze(2),
                         msc[:, :].rearrange("p (h s) -> p h s", s=64),
                         ALU.add, wr=[dots.r], rd=[msc.r])
        self.bflush()
        ph.close()

        for hh in range(2):
            ph = ExitStack()
            wv = self.sbuf(ph, tag + str(hh) + "f2_wv", [128, DC * 1024], BF16)
            wg = self.sbuf(ph, tag + str(hh) + "f2_wg", [128, DC * 1024], BF16)
            self.dma("gpsimd", wv[:, :].rearrange("p (kc z) -> p kc z", z=1024),
                     self.din["wv"].t[:, hh * 1024:(hh + 1) * 1024]
                     .rearrange("(kc p) w -> p kc w", p=128),
                     wr=[wv.r], rd=[self.din["wv"].r], cast=True)
            self.dma("gpsimd", wg[:, :].rearrange("p (kc z) -> p kc z", z=1024),
                     self.din["wg"].t[:, hh * 1024:(hh + 1) * 1024]
                     .rearrange("(kc p) w -> p kc w", p=128),
                     wr=[wg.r], rd=[self.din["wg"].r], cast=True)
            xcs2 = [self.sbuf(ph, tag + str(hh) + f"f2_xc{i}", [128, DC * 128], BF16)
                    for i in range(4)]
            rwt = [self.sbuf(ph, tag + str(hh) + f"f2_rw{i}", [128, 1024], F32)
                   for i in range(2)]
            for rr in range(2):
                self.dma("sync", rwt[rr][:, :],
                         self.din["rw%d" % rr].t[:, hh * 1024:(hh + 1) * 1024],
                         wr=[rwt[rr].r], rd=[self.din["rw%d" % rr].r])
            vb = [self.sbuf(ph, tag + str(hh) + f"f2_v{i}", [128, 1024], F32) for i in range(2)]
            sg = [self.sbuf(ph, tag + str(hh) + f"f2_sg{i}", [128, 1024], BF16) for i in range(2)]
            ob = [self.sbuf(ph, tag + str(hh) + f"f2_o{i}", [128, 1024], F32) for i in range(2)]
            osc = self.sbuf(ph, tag + str(hh) + "f2_os", [128, 1024], F32)
            msc2 = self.sbuf(ph, tag + str(hh) + "f2_ms", [128, 1024], F32)
            gt = [self.sbuf(ph, tag + str(hh) + f"f2_gt{i}", [128, 1024], BF16) for i in range(2)]
            lg = self.sbuf(ph, tag + str(hh) + "f2_lg", [128, 32], F32)
            rt = self.sbuf(ph, tag + str(hh) + "f2_rt", [128, 64], F32)
            for tc in range(16):
                for l in range(2):
                    x_ = xcs2[(tc * 2 + l) % 4]
                    self.dma("sync", x_[:, :]
                             .rearrange("p (kc t) -> p kc t", t=128),
                             xc.t[l][:, :, tc * 128:(tc + 1) * 128]
                             .rearrange("kc p t -> p kc t"),
                             wr=[x_.r], rd=[xc.r])
                    steps = []
                    for kc in range(DC):
                        steps.append(("ldw", x_[:, kc * 128:(kc + 1) * 128]))
                        for n in range(2):
                            bap, _ = self.bank(l * 4 + n)
                            steps.append((bap, x_[:, kc * 128:(kc + 1) * 128],
                                          wv[:, kc * 1024 + n * 512:
                                             kc * 1024 + (n + 1) * 512],
                                          kc == 0, kc == DC - 1))
                        for n in range(2):
                            bap, _ = self.bank(l * 4 + 2 + n)
                            steps.append((bap, x_[:, kc * 128:(kc + 1) * 128],
                                          wg[:, kc * 1024 + n * 512:
                                             kc * 1024 + (n + 1) * 512],
                                          kc == 0, kc == DC - 1))
                    self.mm(steps, wr=self.bankres[l * 4:l * 4 + 4],
                            rd=[x_.r, wv.r, wg.r])
                    pv, pvr = self.banks(l * 4, 2)
                    pg, pgr = self.banks(l * 4 + 2, 2)
                    self.copy(vb[l][:, :], pv, wr=[vb[l].r], rd=pvr)
                    self.act(sg[l][:, :], pg, AF.Silu, wr=[sg[l].r], rd=pgr)
                    for rr in range(2):
                        rw = rwt[rr]
                        self.tt(msc2[:, :], vb[l][:, :],
                                rw[:, :], ALU.mult,
                                wr=[msc2.r], rd=[vb[l].r, rw.r])
                        self.red(lg[:, l * 16 + rr * 8:l * 16 + (rr + 1) * 8]
                                 .unsqueeze(2),
                                 msc2[:, :].rearrange("p (h s) -> p h s", s=128),
                                 ALU.add, wr=[lg.r], rd=[msc2.r])
                d0, d1 = rt[:, 0:8], rt[:, 8:16]
                s0, s1 = rt[:, 16:24], rt[:, 24:32]
                m0, m1 = rt[:, 32:40], rt[:, 40:48]
                eq, c0 = rt[:, 48:56], rt[:, 56:64]
                self.tt(d0, lg[:, 0:8], lg[:, 8:16], ALU.subtract,
                        wr=[rt.r], rd=[lg.r])
                self.tt(d1, lg[:, 16:24], lg[:, 24:32], ALU.subtract,
                        wr=[rt.r], rd=[lg.r])
                self.act(s0, d0, AF.Abs, wr=[rt.r], rd=[rt.r])
                self.act(s0, s0, AF.Sigmoid, wr=[rt.r], rd=[rt.r])
                self.act(s1, d1, AF.Abs, wr=[rt.r], rd=[rt.r])
                self.act(s1, s1, AF.Sigmoid, wr=[rt.r], rd=[rt.r])
                self.ts(m0, d0, 0.0, None, ALU.is_ge, ALU.bypass,
                        wr=[rt.r], rd=[rt.r])
                self.ts(m1, d1, 0.0, None, ALU.is_ge, ALU.bypass,
                        wr=[rt.r], rd=[rt.r])
                self.tt(eq, m0, m1, ALU.mult, wr=[rt.r], rd=[rt.r])
                self.stt(eq, eq, 2.0, m0, ALU.mult, ALU.subtract,
                         wr=[rt.r], rd=[rt.r])
                self.tt(eq, eq, m1, ALU.subtract, wr=[rt.r], rd=[rt.r])
                self.ts(eq, eq, 1.0, None, ALU.add, ALU.bypass,
                        wr=[rt.r], rd=[rt.r])
                base = tc * 48 + hh * 8
                dt00 = dots[:, base:base + 8]
                dt01 = dots[:, base + 16:base + 24]
                dt11 = dots[:, base + 32:base + 40]
                self.stt(c0, s0, 0.125, dt00, ALU.mult, ALU.mult,
                         wr=[rt.r], rd=[rt.r, dots.r])
                bc = lambda a: a.unsqueeze(2).broadcast_to([128, 8, 128])
                o0v = ob[0][:, :].rearrange("p (h s) -> p h s", s=128)
                o1v = ob[1][:, :].rearrange("p (h s) -> p h s", s=128)
                v0v = vb[0][:, :].rearrange("p (h s) -> p h s", s=128)
                v1v = vb[1][:, :].rearrange("p (h s) -> p h s", s=128)
                self.tt(o0v, v0v, bc(c0), ALU.mult, wr=[ob[0].r],
                        rd=[vb[0].r, rt.r])
                self.ts(rt[:, 0:8], eq, -0.5, 1.0, ALU.mult, ALU.add,
                        wr=[rt.r], rd=[rt.r])
                self.tt(rt[:, 8:16], s1, rt[:, 0:8], ALU.mult,
                        wr=[rt.r], rd=[rt.r])
                self.tt(c0, rt[:, 8:16], eq, ALU.mult, wr=[rt.r], rd=[rt.r])
                self.stt(c0, c0, 0.125, dt01, ALU.mult, ALU.mult,
                         wr=[rt.r], rd=[rt.r, dots.r])
                self.tt(o1v, v0v, bc(c0), ALU.mult, wr=[ob[1].r],
                        rd=[vb[0].r, rt.r])
                self.stt(c0, rt[:, 8:16], 0.125, dt11, ALU.mult, ALU.mult,
                         wr=[rt.r], rd=[rt.r, dots.r])
                self.tt(osc[:, :].rearrange("p (h s) -> p h s", s=128),
                        v1v, bc(c0), ALU.mult, wr=[osc.r], rd=[vb[1].r, rt.r])
                self.tt(ob[1][:, :], ob[1][:, :], osc[:, :], ALU.add,
                        wr=[ob[1].r], rd=[ob[1].r, osc.r])
                for l in range(2):
                    ov = (o0v if l == 0 else o1v)
                    orr = ob[l].r
                    mn, e2 = rt[:, 0:8], rt[:, 8:16]
                    self.red(mn.unsqueeze(2), ov, ALU.add, wr=[rt.r], rd=[orr])
                    self.ts(mn, mn, 1.0 / 128.0, None, ALU.mult, ALU.bypass,
                            wr=[rt.r], rd=[rt.r])
                    self.act(osc[:, :], ob[l][:, :], AF.Square,
                             wr=[osc.r], rd=[orr])
                    self.red(e2.unsqueeze(2),
                             osc[:, :].rearrange("p (h s) -> p h s", s=128),
                             ALU.add, wr=[rt.r], rd=[osc.r])
                    self.tt(rt[:, 16:24], mn, mn, ALU.mult, wr=[rt.r], rd=[rt.r])
                    self.stt(e2, e2, 1.0 / 128.0, rt[:, 16:24], ALU.mult,
                             ALU.subtract, wr=[rt.r], rd=[rt.r])
                    self.act(e2, e2, AF.Ln, bias=1e-5, wr=[rt.r], rd=[rt.r])
                    self.act(e2, e2, AF.Exp, scale=-0.5, wr=[rt.r], rd=[rt.r])
                    self.tt(ov, ov, bc(mn), ALU.subtract, wr=[orr], rd=[orr, rt.r])
                    self.tt(ov, ov, bc(e2), ALU.mult, wr=[orr], rd=[orr, rt.r])
                    self.tt(gt[l][:, :], sg[l][:, :], ob[l][:, :], ALU.mult,
                            wr=[gt[l].r], rd=[sg[l].r, orr])
                    self.dma("sync",
                             gated.t[l, tc * 128:(tc + 1) * 128,
                                     hh * 1024:(hh + 1) * 1024],
                             gt[l][:, :], wr=[gated.r], rd=[gt[l].r])
            self.bflush()
            ph.close()

        ph = ExitStack()
        gfm = self.sbuf(ph, tag + "fw_g", [128, DC * TC], BF16)
        wo = self.sbuf(ph, tag + "fw_w", [128, DC * 2048], BF16)
        self.dma("gpsimd", wo[:, :].rearrange("p (kc z) -> p kc z", z=2048),
                 self.din["wof"].t[...].rearrange("(kc p) w -> p kc w", p=128),
                 wr=[wo.r], rd=[self.din["wof"].r], cast=True)
        xpt = [self.sbuf(ph, tag + f"fw_x{i}", [128, D], F32) for i in range(2)]
        xnb = [self.sbuf(ph, tag + f"fw_xn{i}", [128, D], F32) for i in range(2)]
        ynb = [self.sbuf(ph, tag + f"fw_yn{i}", [128, D], BF16) for i in range(2)]
        scr = self.sbuf(ph, tag + "fw_sc", [128, D], F32)
        mu = self.sbuf(ph, tag + "fw_mu", [128, 1], F32)
        rstd = self.sbuf(ph, tag + "fw_rs", [128, 1], F32)
        tm1 = self.sbuf(ph, tag + "fw_t1", [128, 1], F32)
        for l in range(2):
            for dc in range(DC):
                self.dma("sync", gfm[:, dc * TC:(dc + 1) * TC],
                         gated.t[l, :, dc * 128:(dc + 1) * 128],
                         wr=[gfm.r], rd=[gated.r], transpose=True)
            for tc in range(16):
                bq = 0 if tc % 2 == 0 else 4
                steps = []
                for kc in range(DC):
                    steps.append(("ldw", gfm[:, kc * TC + tc * 128:
                                         kc * TC + (tc + 1) * 128]))
                    for n in range(4):
                        bap, _ = self.bank(bq + n)
                        steps.append((bap, gfm[:, kc * TC + tc * 128:
                                             kc * TC + (tc + 1) * 128],
                                      wo[:, kc * 2048 + n * 512:
                                         kc * 2048 + (n + 1) * 512],
                                      kc == 0, kc == DC - 1))
                self.mm(steps, wr=self.bankres[bq:bq + 4], rd=[gfm.r, wo.r])
                rows = slice(l * TC + tc * 128, l * TC + (tc + 1) * 128)
                xp = xpt[tc % 2]
                xn = xnb[tc % 2]
                yn = ynb[tc % 2]
                self.dma("sync", xp[:, :], xdr.t[rows, :], wr=[xp.r],
                         rd=[xdr.r])
                pap, pres = self.banks(bq, 4)
                self.tt(xn[:, :], pap, xp[:, :], ALU.add, wr=[xn.r],
                        rd=pres + [xp.r])
                self.dma("sync", xodr.t[rows, :], xn[:, :], wr=[xodr.r],
                         rd=[xn.r])
                if yodr is not None:
                    self.lnstats(xn[:, :], [xn.r], scr, mu, rstd, tm1)
                    self.ts(yn[:, :], xn[:, :], mu[:, 0:1], rstd[:, 0:1],
                            ALU.subtract, ALU.mult, wr=[yn.r],
                            rd=[xn.r, mu.r, rstd.r])
                    self.dma("sync", yodr.t[rows, :], yn[:, :], wr=[yodr.r],
                             rd=[yn.r])
        return ph

    def phase_attnb_proj(self):
        g = self.g
        y2 = self.dr["Y2"]
        ph = ExitStack()
        y2fm = self.sbuf(ph, "c1_yfm", [128, DC * TC], BF16)
        wqkc = self.sbuf(ph, "c1_wqk", [128, DC * 128], BF16)
        wvq = [self.sbuf(ph, f"c1_wv{i}", [128, DC * 1024], BF16)
               for i in range(2)]
        wgc = [self.sbuf(ph, f"c1_wg{i}", [128, DC * 128], BF16)
               for i in range(2)]
        qks = self.sbuf(ph, "c1_qks", [128, TC], F32)
        qbf = self.sbuf(ph, "c1_q", [128, TC], BF16)
        kbf = self.sbuf(ph, "c1_k", [128, TC], BF16)
        bhv = self.sbuf(ph, "c1_bhv", [128, 1024], F32)
        vsc = [self.sbuf(ph, f"c1_vs{i}", [128, 1024], F32) for i in range(2)]
        vbf = [self.sbuf(ph, f"c1_vb{i}", [128, 1024], BF16) for i in range(2)]
        gbf = [self.sbuf(ph, f"c1_gb{i}", [128, TC], BF16) for i in range(2)]

        self.dma("gpsimd", wqkc[:, :].rearrange("p (kc w) -> p kc w", w=128),
                 self.din["wqkb"].t[...].rearrange("(kc p) w -> p kc w", p=128),
                 wr=[wqkc.r], rd=[self.din["wqkb"].r], cast=True)

        for b in range(B):
            for dc in range(DC):
                self.dma("sync", y2fm[:, dc * TC:(dc + 1) * TC],
                         y2.t[b * TC:(b + 1) * TC, dc * 128:(dc + 1) * 128],
                         wr=[y2fm.r], rd=[y2.r], transpose=True)
            steps = []
            for kc in range(DC):
                steps.append(("ldw", wqkc[:, kc * 128:(kc + 1) * 128]))
                for n in range(4):
                    bap, _ = self.bank(n)
                    steps.append((bap, wqkc[:, kc * 128:(kc + 1) * 128],
                                  y2fm[:, kc * TC + n * 512:
                                       kc * TC + (n + 1) * 512],
                                  kc == 0, kc == DC - 1))
            self.mm(steps, wr=self.bankres[0:4], rd=[wqkc.r, y2fm.r])
            pap, pres = self.banks(0, 4)
            self.act(qks[:, :], pap, AF.Silu, bias=g["bqk"][:, 0:1],
                     wr=[qks.r], rd=pres + [g["bqk"].r])
            self.ts(qbf[:, :], qks[:, :], g["gq"][:, 0:1], g["bq"][:, 0:1],
                    ALU.mult, ALU.add, wr=[qbf.r],
                    rd=[qks.r, g["gq"].r, g["bq"].r])
            self.ts(kbf[:, :], qks[:, :], g["gk"][:, 0:1], g["bk"][:, 0:1],
                    ALU.mult, ALU.add, wr=[kbf.r],
                    rd=[qks.r, g["gk"].r, g["bk"].r])
            self.dma("sync", self.dr["QLOC"].t[b], qbf[:, :],
                     wr=[self.dr["QLOC"].r], rd=[qbf.r])
            self.dma("sync", self.dr["KLOC"].t[b], kbf[:, :],
                     wr=[self.dr["KLOC"].r], rd=[kbf.r])
            for qd in range(4):
                wv = wvq[qd % 2]
                self.dma("gpsimd",
                         wv[:, :].rearrange("p (kc z) -> p kc z", z=1024),
                         self.din["whid"].t[:, qd * 1024:(qd + 1) * 1024]
                         .rearrange("(kc p) w -> p kc w", p=128),
                         wr=[wv.r], rd=[self.din["whid"].r], cast=True)
                self.rowbc(bhv[:, :], [bhv.r],
                           self.din["bhid"].t[0:1, qd * 1024:(qd + 1) * 1024],
                           [self.din["bhid"].r], 1024)
                vloc = self.dr[f"VLOC{qd}"]
                for tc in range(16):
                    bq = 4 + 2 * (tc % 2)
                    vi = tc % 2
                    steps = []
                    for kc in range(DC):
                        steps.append(("ldw", y2fm[:, kc * TC + tc * 128:
                                             kc * TC + (tc + 1) * 128]))
                        for n in range(2):
                            bap, _ = self.bank(bq + n)
                            steps.append((bap, y2fm[:, kc * TC + tc * 128:
                                                 kc * TC + (tc + 1) * 128],
                                          wv[:, kc * 1024 + n * 512:
                                             kc * 1024 + (n + 1) * 512],
                                          kc == 0, kc == DC - 1))
                    self.mm(steps, wr=self.bankres[bq:bq + 2],
                            rd=[y2fm.r, wv.r])
                    pap2, pres2 = self.banks(bq, 2)
                    self.tt(vsc[vi][:, :], pap2, bhv[:, :], ALU.add,
                            wr=[vsc[vi].r], rd=pres2 + [bhv.r])
                    self.act(vbf[vi][:, :], vsc[vi][:, :], AF.Silu,
                             wr=[vbf[vi].r], rd=[vsc[vi].r])
                    self.dma("sync",
                             vloc.t[b * TC + tc * 128:b * TC + (tc + 1) * 128, :],
                             vbf[vi][:, :], wr=[vloc.r], rd=[vbf[vi].r])
            for gc in range(32):
                wg = wgc[gc % 2]
                self.dma("gpsimd",
                         wg[:, :].rearrange("p (kc w) -> p kc w", w=128),
                         self.din["whid"].t[:, 4096 + gc * 128:
                                            4096 + (gc + 1) * 128]
                         .rearrange("(kc p) w -> p kc w", p=128),
                         wr=[wg.r], rd=[self.din["whid"].r], cast=True)
                bq = 4 * (gc % 2)
                steps = []
                for kc in range(DC):
                    steps.append(("ldw", wg[:, kc * 128:(kc + 1) * 128]))
                    for n in range(4):
                        bap, _ = self.bank(bq + n)
                        steps.append((bap, wg[:, kc * 128:(kc + 1) * 128],
                                      y2fm[:, kc * TC + n * 512:
                                           kc * TC + (n + 1) * 512],
                                      kc == 0, kc == DC - 1))
                self.mm(steps, wr=self.bankres[bq:bq + 4], rd=[wg.r, y2fm.r])
                pap3, pres3 = self.banks(bq, 4)
                gi = gc % 2
                self.act(gbf[gi][:, :], pap3, AF.Silu,
                         bias=g["bhg"][:, gc:gc + 1],
                         wr=[gbf[gi].r], rd=pres3 + [g["bhg"].r])
                self.dma("sync", self.dr["GATEC"].t[b, gc], gbf[gi][:, :],
                         wr=[self.dr["GATEC"].r], rd=[gbf[gi].r])
        kl, kf = self.dr["KLOC"], self.dr["KFULL"]
        self.tk.coll(lambda e, i=kl.t[...].rearrange("b p t -> (b p) t"),
                     o=kf.t[...]:
                     e.collective_compute("AllGather", ALU.bypass,
                                          replica_groups=[list(range(NCORE))],
                                          ins=[i], outs=[o]),
                     reads=[kl.r], writes=[kf.r])
        for qd in range(4):
            vl, vf = self.dr[f"VLOC{qd}"], self.dr[f"VFULL{qd}"]
            self.tk.coll(lambda e, i=vl.t[...], o=vf.t[...]:
                         e.collective_compute("AllGather", ALU.bypass,
                                              replica_groups=[list(range(NCORE))],
                                              ins=[i], outs=[o]),
                         reads=[vl.r], writes=[vf.r])
        return ph

    def phase_attnb_core(self):
        ph = ExitStack()
        oacc = self.sbuf(ph, "c3_oac", [128, 32 * 512], F32)
        atb = self.sbuf(ph, "c3_at", [128, 32 * 512], BF16)
        vcb = [self.sbuf(ph, f"c3_v{i}", [128, 8 * 512], BF16) for i in range(4)]
        kb4 = self.sbuf(ph, "c3_k", [128, 4096], BF16)
        qblk = self.sbuf(ph, "c3_q", [128, 512], BF16)
        rels = [self.sbuf(ph, f"c3_rl{i}", [128, 2048], F32) for i in range(2)]
        gfmc = self.sbuf(ph, "c3_gf", [128, 32 * 512], BF16)
        gtd = [self.sbuf(ph, f"c3_gt{i}", [128, 2048], BF16) for i in range(2)]
        kf = self.dr["KFULL"]
        ql = self.dr["QLOC"]
        for b in range(B):
            for ihq in range(4):
                isl = slice(ihq * 512, (ihq + 1) * 512)
                self.dma("sync", qblk[:, :], ql.t[b][:, isl], wr=[qblk.r],
                         rd=[ql.r])
                for jjb in range(4):
                    for h in range(2):
                        rr = 2 * jjb + h
                        self.dma("sync", kb4[:, h * 2048:(h + 1) * 2048],
                                 kf.t[rr * 256 + b * 128:
                                      rr * 256 + (b + 1) * 128, :],
                                 wr=[kb4.r], rd=[kf.r])
                    for jg in range(8):
                        sq = 0 if jg % 2 == 0 else 4
                        steps = []
                        for s in range(4):
                            jc = jg * 4 + s
                            bap, _ = self.bank(sq + s)
                            steps.append(("ldw", kb4[:, jc * 128:(jc + 1) * 128]))
                            steps.append((bap, kb4[:, jc * 128:(jc + 1) * 128],
                                          qblk[:, :], True, True))
                        self.mm(steps, wr=self.bankres[sq:sq + 4],
                                rd=[kb4.r, qblk.r])
                        pap, pres = self.banks(sq, 4)
                        rl = rels[0] if jg % 2 == 0 else rels[1]
                        self.act(rl[:, :], pap, AF.Relu, scale=1.0 / T,
                                 wr=[rl.r], rd=pres)
                        self.tt(atb[:, jg * 2048:(jg + 1) * 2048], rl[:, :],
                                rl[:, :], ALU.mult, wr=[atb.r], rd=[rl.r])
                    for hqg in range(8):
                        bq = 4 if hqg % 2 == 0 else 0
                        for jcg in range(4):
                            vb = vcb[jcg]
                            qd = hqg // 2
                            hoff = (hqg % 2) * 512
                            rr = 2 * jjb + jcg // 2
                            t0 = (jcg % 2) * 1024
                            self.dma("sync",
                                     vb[:, :].rearrange("p (jc h) -> p jc h",
                                                        h=512),
                                     self.dr[f"VFULL{qd}"]
                                     .t[rr * R + b * TC + t0:
                                        rr * R + b * TC + t0 + 1024,
                                        hoff:hoff + 512]
                                     .rearrange("(jc p) h -> p jc h", p=128),
                                     wr=[vb.r], rd=[self.dr[f"VFULL{qd}"].r])
                            steps = []
                            for jc8 in range(8):
                                jc = jcg * 8 + jc8
                                for h4 in range(4):
                                    bap, _ = self.bank(bq + h4)
                                    steps.append(
                                        ("ldw", vb[:, jc8 * 512 + h4 * 128:
                                                   jc8 * 512 + (h4 + 1) * 128]))
                                    steps.append(
                                        (bap, vb[:, jc8 * 512 + h4 * 128:
                                                 jc8 * 512 + (h4 + 1) * 128],
                                         atb[:, jc * 512:(jc + 1) * 512],
                                         jcg == 0 and jc8 == 0,
                                         jcg == 3 and jc8 == 7))
                            self.mm(steps, wr=self.bankres[bq:bq + 4],
                                    rd=[vb.r, atb.r])
                        pap, pres = self.banks(bq, 4)
                        osl = oacc[:, hqg * 2048:(hqg + 1) * 2048]
                        if jjb == 0:
                            self.copy(osl, pap, wr=[oacc.r], rd=pres)
                        else:
                            self.stt(osl, pap, 1.0, osl, ALU.mult, ALU.add,
                                     wr=[oacc.r], rd=pres + [oacc.r])
                self.dma("sync",
                         gfmc[:, :].rearrange("p (gc i) -> p gc i", i=512),
                         self.dr["GATEC"].t[b][:, :, isl]
                         .rearrange("gc p i -> p gc i"),
                         wr=[gfmc.r], rd=[self.dr["GATEC"].r])
                for cc in range(8):
                    c0 = cc * 2048
                    gt_ = gtd[cc % 2]
                    if True:
                        self.tt(gt_[:, :], oacc[:, c0:c0 + 2048],
                                gfmc[:, c0:c0 + 2048], ALU.mult,
                                wr=[gt_.r], rd=[oacc.r, gfmc.r])
                        gc0 = c0 // 512
                        self.dma("sync",
                                 self.dr["GTDC"].t[b][gc0:gc0 + 4, :, isl]
                                 .rearrange("gc p i -> p gc i"),
                                 gt_[:, :].rearrange("p (gc i) -> p gc i",
                                                     i=512),
                                 wr=[self.dr["GTDC"].r], rd=[gt_.r])
        return ph

    def phase_attnb_out(self):
        g = self.g
        ph = ExitStack()
        wo = self.sbuf(ph, "c4_wo", [128, 32 * 2048], BF16)
        gch = [self.sbuf(ph, f"c4_g{i}", [128, 32 * 128], BF16)
               for i in range(2)]
        bobc = self.sbuf(ph, "c4_bo", [128, D], F32)
        x2t = [self.sbuf(ph, "c4_x0", [128, D], F32)] * 2
        x3b = [self.sbuf(ph, "c4_x30", [128, D], F32)] * 2
        y3b = [self.sbuf(ph, "c4_y30", [128, D], BF16)] * 2
        s1 = self.sbuf(ph, "c4_s1", [128, D], F32)
        scr = self.sbuf(ph, "c4_sc", [128, D], F32)
        mu = self.sbuf(ph, "c4_mu", [128, 1], F32)
        rstd = self.sbuf(ph, "c4_rs", [128, 1], F32)
        tm1 = self.sbuf(ph, "c4_t1", [128, 1], F32)
        self.dma("gpsimd", wo[:, :].rearrange("p (kc z) -> p kc z", z=2048),
                 self.din["wob"].t[...].rearrange("(kc p) w -> p kc w", p=128),
                 wr=[wo.r], rd=[self.din["wob"].r], cast=True)
        self.rowbc(bobc[:, :], [bobc.r], self.din["bob"].t[...],
                   [self.din["bob"].r], D)
        x2d, x3d, y2d, y3d = (self.dr["X2"], self.dr["X3"], self.dr["Y2"],
                              self.dr["Y3"])
        for b in range(B):
            for tc in range(16):
                gc_ = gch[tc % 2]
                self.dma("sync",
                         gc_[:, :].rearrange("p (kc t) -> p kc t", t=128),
                         self.dr["GTDC"].t[b][:, :, tc * 128:(tc + 1) * 128]
                         .rearrange("gc p t -> p gc t"),
                         wr=[gc_.r], rd=[self.dr["GTDC"].r])
                bq = 0 if tc % 2 == 0 else 4
                steps = []
                for kc in range(32):
                    steps.append(("ldw", gc_[:, kc * 128:(kc + 1) * 128]))
                    for n in range(4):
                        bap, _ = self.bank(bq + n)
                        steps.append((bap, gc_[:, kc * 128:(kc + 1) * 128],
                                      wo[:, kc * 2048 + n * 512:
                                         kc * 2048 + (n + 1) * 512],
                                      kc == 0, kc == 31))
                self.mm(steps, wr=self.bankres[bq:bq + 4], rd=[gc_.r, wo.r])
                rows = slice(b * TC + tc * 128, b * TC + (tc + 1) * 128)
                xp = x2t[tc % 2]
                x3 = x3b[tc % 2]
                y3 = y3b[tc % 2]
                self.dma("sync", xp[:, :], x2d.t[rows, :], wr=[xp.r],
                         rd=[x2d.r])
                pap, pres = self.banks(bq, 4)
                self.tt(s1[:, :], pap, bobc[:, :], ALU.add,
                        wr=[s1.r], rd=pres + [bobc.r])
                self.lnstats(xp[:, :], [xp.r], scr, mu, rstd, tm1)
                self.ts(x3[:, :], xp[:, :], mu[:, 0:1], rstd[:, 0:1],
                        ALU.subtract, ALU.mult, wr=[x3.r],
                        rd=[xp.r, mu.r, rstd.r])
                self.tt(s1[:, :], s1[:, :], x3[:, :], ALU.add,
                        wr=[s1.r], rd=[s1.r, x3.r])
                self.tt(x3[:, :], s1[:, :], xp[:, :], ALU.add,
                        wr=[x3.r], rd=[s1.r, xp.r])
                self.dma("sync", x3d.t[rows, :], x3[:, :], wr=[x3d.r],
                         rd=[x3.r])
                self.lnstats(x3[:, :], [x3.r], scr, mu, rstd, tm1)
                self.ts(y3[:, :], x3[:, :], mu[:, 0:1], rstd[:, 0:1],
                        ALU.subtract, ALU.mult, wr=[y3.r],
                        rd=[x3.r, mu.r, rstd.r])
                self.dma("sync", y3d.t[rows, :], y3[:, :], wr=[y3d.r],
                         rd=[y3.r])
        return ph

    def build(self, block, phases=("s", "0", "a1", "a2", "a3", "b", "c1",
                                   "c3", "c4", "d")):
        self.block = block

        def run(fn):
            st = fn()
            self.bflush()
            if st is not None:
                st.close()
        if "s" in phases:
            run(self.phase_setup)
        if "0" in phases and "a1" in phases:
            st1 = self.phase_ln0()
            st2 = self.phase_wsplit()
            self.bflush()
            st2.close()
            st1.close()
            run(self.phase_a1pre)
            run(self.phase_a1)
        elif "0" in phases:
            run(self.phase_ln0)
        elif "a1" in phases:
            run(self.phase_wsplit)
            run(self.phase_a1pre)
            run(self.phase_a1)
        if "a2" in phases:
            run(self.phase_a2)
        if "a3" in phases:
            run(self.phase_a3)
        if "b" in phases:
            run(lambda: self.phase_ffn("Y1", "X1", "X2", "Y2", tag="B"))
        if "c1" in phases:
            run(self.phase_attnb_proj)
        if "c3" in phases:
            run(self.phase_attnb_core)
        if "c4" in phases:
            run(self.phase_attnb_out)
        if "d" in phases:
            run(lambda: self.phase_ffn("Y3", "X3", "out", None, tag="Dd"))
        for name in self.dev:
            self.dump(name)
        self.tk.barrier()
        self.tk.final_wait()
        self.tk.flush(block)


def build_program(dev=(), phases=None):
    gk = GK(dev=dev)
    gk.declare()
    with gk.es:
        block = gk.es.enter_context(gk.nc.Block())
        if phases is None:
            gk.build(block)
        else:
            gk.build(block, phases)
    return gk


def kernel(**inputs):
    in_maps = prep_inputs(inputs)
    gk = build_program()
    res = run_bass_kernel_spmd(gk.nc, in_maps, core_ids=list(range(NCORE)))
    parts = [np.asarray(r["out"]).reshape(B, TC, D) for r in res.results]
    return np.concatenate(parts, axis=1)
